# version 01
# baseline (speedup 1.0000x reference)
import numpy as np
import ml_dtypes
from contextlib import ExitStack

import concourse.bass as bass
import concourse.mybir as mybir
from concourse.bass_utils import run_bass_kernel_spmd

F32 = mybir.dt.float32
BF16 = mybir.dt.bfloat16
AF = mybir.ActivationFunctionType
ALU = mybir.AluOpType
AX = mybir.AxisListType
BF = ml_dtypes.bfloat16

T = 4096
DM = 1024
TC = 512
NCH = T // TC
NP = DM // 128
HID = 2730
NJ = 22

ENGINES = ("tensor", "vector", "scalar", "gpsimd", "sync")



def rotary_tables(window):
    n, dh = T, 64
    half = np.arange(0, dh, 2, dtype=np.float64)
    inv_freq = 1.0 / 10000.0 ** (half / dh)
    t = np.arange(n, dtype=np.float64)
    freqs = np.concatenate([t[:, None] * inv_freq[None, :]] * 2, 1)
    base = (half + 0.4 * dh) / (1.4 * dh)
    power = (t - n // 2) / (window // 2)
    scale = np.concatenate([base[None, :] ** power[:, None]] * 2, 1)
    cos, sin = np.cos(freqs), np.sin(freqs)
    tabs = np.stack([cos * scale, sin * scale, cos / scale, sin / scale])
    tabs = tabs.transpose(0, 2, 1)
    tabs = np.concatenate([tabs, tabs], axis=1)
    return np.ascontiguousarray(tabs.astype(np.float32))


def const_pack():
    c = {}
    c["ident"] = np.eye(128, dtype=np.float32)
    rt = np.zeros((64, 64), np.float32)
    for d in range(32):
        rt[d + 32, d] = -1.0
        rt[d, d + 32] = 1.0
    rmt = np.zeros((128, 128), np.float32)
    rmt[0:64, 0:64] = rt
    rmt[64:128, 64:128] = rt
    c["rmatT"] = rmt
    r = np.arange(128)[:, None]
    col = np.arange(128)[None, :]
    NEG = np.float32(-1e30)
    masks = np.zeros((128, 256), np.float32)
    masks[:, 0:128] = np.where(col <= r, 0.0, NEG)
    masks[:, 128:256] = np.where(col >= r, 0.0, NEG)
    c["masks"] = masks
    sp = np.zeros((128, 2), np.float32)
    sp[0:64, 0] = 1.0
    sp[64:128, 1] = 1.0
    c["selpair"] = sp
    s2 = np.zeros((2, 128), np.float32)
    s2[0, 0:64] = 1.0
    s2[1, 64:128] = 1.0
    c["sel2bc"] = s2
    c["ones_col"] = np.ones((128, 1), np.float32)
    eps = np.zeros((128, 4), np.float32)
    eps[:, 0] = 1e-5
    eps[:, 1] = 1e-6
    eps[:, 2] = 1.0
    eps[:, 3] = 1e-4
    c["epscol"] = eps
    c["id32"] = np.eye(32, dtype=np.float32)
    return c


def pack_mamba(W_in, W_conv, b_conv, W_x, W_dt, b_dt, A_log, D, rms_w, W_out):
    m = {}
    m["winT"] = np.ascontiguousarray(W_in.T.astype(np.float32))
    m["woutT"] = np.ascontiguousarray(W_out.T.astype(np.float32))
    m["wxT"] = np.ascontiguousarray(W_x.T.astype(np.float32))
    m["wdtT"] = np.ascontiguousarray(W_dt.T.astype(np.float32))
    m["alog"] = np.ascontiguousarray(
        A_log.reshape(16, 128, 16).transpose(1, 0, 2).reshape(128, 256).astype(np.float32))
    ms = np.zeros((128, 144), np.float32)
    ms[:, 0:16] = b_conv.reshape(16, 128).T
    wc = W_conv.reshape(16, 128, 4)
    for d_ in range(16):
        for k_ in range(4):
            ms[:, 16 + 4 * d_ + k_] = wc[d_, :, k_]
    ms[:, 80:96] = b_dt.reshape(16, 128).T
    ms[:, 96:112] = D.reshape(16, 128).T
    ms[:, 112:128] = rms_w.reshape(16, 128).T
    m["msml"] = ms
    return m


def pack_local_smalls(qs, ks, b1, b2):
    depth = qs.shape[0]
    out = np.zeros((depth, 128, 54), np.float32)
    for l in range(depth):
        out[l, :, 0] = np.tile(np.asarray(qs[l]), 2)
        out[l, :, 1] = np.tile(np.asarray(ks[l]), 2)
        for j in range(NJ):
            a = b1[l, 128 * j: min(128 * (j + 1), HID)]
            out[l, : len(a), 2 + j] = a
            g = b1[l, HID + 128 * j: HID + min(128 * (j + 1), HID)]
            out[l, : len(g), 24 + j] = g
        for p in range(NP):
            out[l, :, 46 + p] = b2[l, 128 * p: 128 * (p + 1)]
    return out



class Op:
    __slots__ = ("eng", "fn", "reads", "writes", "deps", "inc_idx",
                 "stream", "n", "waits", "slot", "cycle")

    def __init__(self):
        self.inc_idx = None
        self.stream = None


class Stream:
    def __init__(self, name, eng):
        self.name = name
        self.eng = eng
        self.count = 0
        self.sem = None


class Emit:
    def __init__(self):
        self.ops = []
        self.lastw = {}
        self.readers = {}
        self.streams = {}

    def stream(self, name, eng):
        if name not in self.streams:
            self.streams[name] = Stream(name, eng)
        return self.streams[name]

    def op(self, eng, fn, reads=(), writes=(), stream=None):
        o = Op()
        o.eng = eng
        o.fn = fn
        o.reads = tuple(reads)
        o.writes = tuple(writes)
        o.stream = stream
        o.n = len(self.ops)
        deps = set()
        for r in o.reads:
            w = self.lastw.get(r)
            if w is not None:
                deps.add(w.n)
        for w_ in o.writes:
            lw = self.lastw.get(w_)
            if lw is not None:
                deps.add(lw.n)
            for rd in self.readers.get(w_, ()):
                deps.add(rd.n)
        deps.discard(o.n)
        o.deps = deps
        for r in o.reads:
            self.readers.setdefault(r, []).append(o)
        for w_ in o.writes:
            self.lastw[w_] = o
            self.readers[w_] = [o]
        self.ops.append(o)
        return o

    def dma(self, stream_name, eng, fn, reads=(), writes=()):
        return self.op(eng, fn, reads, writes, stream=self.stream(stream_name, eng))

    def finalize_and_emit(self, nc, stack, block):
        ops = self.ops
        promoted = set()
        for o in ops:
            for dn in o.deps:
                d = ops[dn]
                if d.stream is None and d.eng == o.eng and d.eng == "tensor" \
                        and o.stream is None:
                    continue
                promoted.add(dn)
        R = 8
        ecount = {e: 0 for e in ENGINES}
        for o in ops:
            if o.stream is not None:
                i = o.stream.count
                o.stream.count += 1
                o.slot = i % R
                o.cycle = i // R
                o.inc_idx = 16 * (o.cycle + 1)
            elif o.n in promoted:
                ecount[o.eng] += 1
                o.inc_idx = ecount[o.eng]
        esem = {e: stack.enter_context(nc.semaphore(f"S{e[:2]}")) for e in ENGINES}
        for s in self.streams.values():
            s.sem = [stack.enter_context(nc.semaphore(f"D{s.name}{i}"))
                     for i in range(R)]
        eng_guar = {e: {} for e in ENGINES}
        op_guar = {}
        stream_prev = {}
        nwaits = 0
        for o in ops:
            g = eng_guar[o.eng]
            waits = {}
            for dn in o.deps:
                d = ops[dn]
                if d.stream is None and d.eng == o.eng and d.eng == "tensor" \
                        and o.stream is None:
                    continue
                key = (("D", d.stream.name, d.slot) if d.stream is not None
                       else ("S", d.eng))
                val = d.inc_idx
                if g.get(key, 0) >= val or waits.get(key, 0) >= val:
                    continue
                waits[key] = val
            if o.stream is not None and o.cycle > 0:
                key = ("D", o.stream.name, o.slot)
                val = 16 * o.cycle
                if g.get(key, 0) < val and waits.get(key, 0) < val:
                    waits[key] = val
            for dn in o.deps:
                dg = op_guar.get(dn)
                if dg:
                    for k, v in dg.items():
                        if g.get(k, 0) < v:
                            g[k] = v
            for k, v in waits.items():
                if g.get(k, 0) < v:
                    g[k] = v
            o.waits = waits
            nwaits += len(waits)
            if o.inc_idx is not None:
                snap = dict(g)
                if o.stream is not None:
                    p = stream_prev.get((o.stream.name, o.slot))
                    if p is not None:
                        for k, v in op_guar[p.n].items():
                            if snap.get(k, 0) < v:
                                snap[k] = v
                    stream_prev[(o.stream.name, o.slot)] = o
                op_guar[o.n] = snap
        semmap = dict((("S", e), esem[e]) for e in ENGINES)
        for s in self.streams.values():
            for i in range(R):
                semmap[("D", s.name, i)] = s.sem[i]
        self.engine_list = {e: [o for o in ops if o.eng == e] for e in ENGINES}

        def emit_engine(eng_name):
            def body(eng):
                for o in self.engine_list[eng_name]:
                    for k, v in sorted(o.waits.items(), key=str):
                        eng.wait_ge(semmap[k], v)
                    ins = o.fn(eng)
                    if o.inc_idx is not None:
                        if o.stream is not None:
                            ins.then_inc(semmap[("D", o.stream.name, o.slot)], 16)
                        else:
                            ins.then_inc(semmap[("S", o.eng)], 1)
            return body

        stats = {e: (len(self.engine_list[e]), ecount[e]) for e in ENGINES}
        print(f"emit: {len(ops)} ops, waits={nwaits}, (ops,incs)={stats}, "
              f"dma={[(s.name, s.count) for s in self.streams.values()]}")
        block.tensor(emit_engine("tensor"))
        block.vector(emit_engine("vector"))
        block.scalar(emit_engine("scalar"))
        block.gpsimd(emit_engine("gpsimd"))
        block.sync(emit_engine("sync"))



class LocalCfg:
    def __init__(self, name, depth, inner, heads, window):
        self.name = name
        self.depth = depth
        self.inner = inner
        self.heads = heads
        self.window = window
        self.nq = inner // 128
        self.nm = 3 * self.nq
        self.L = window // 128
        self.npair = heads // 2


L0F = LocalCfg("l0f", 2, 512, 8, 512)
L1F = LocalCfg("l1f", 6, 1024, 16, 256)


class MK:
    def __init__(self, nc, stack):
        self.nc = nc
        self.stack = stack
        self.em = Emit()
        self.sb_bytes = 0
        self.sbufs = {}
        self.psum_rr = -1

    def sb(self, name, shape, dtype=F32):
        t = self.stack.enter_context(self.nc.sbuf_tensor(name, shape, dtype))
        self.sb_bytes += shape[1] * (4 if dtype == F32 else 2)
        return t

    def alloc_common(self):
        sb = self.sb
        self.wk = [sb(f"wk{p}", [128, 5460], BF16) for p in range(NP)]
        self.xa = [sb(f"xa{p}", [128, TC + 4]) for p in range(NP)]
        self.xab = [sb(f"xab{p}", [128, TC + 4], BF16) for p in range(NP)]
        self.qkf = [sb(f"qkf{i}", [128, TC]) for i in range(2)]
        self.scr = [sb(f"scr{i}", [128, TC]) for i in range(3)]
        self.bfA = [sb(f"bfA{i}", [128, TC], BF16) for i in range(8)]
        self.bfB = [self.xab[p][:, 4:4 + TC] for p in range(NP)]
        self.kr = [sb(f"kr{p}", [128, 1024], BF16) for p in range(8)]
        self.vtm = [sb(f"vtm{s}", [128, 1024], BF16) for s in range(6)]
        self.pT = [sb(f"pT{i}", [128, 128], BF16) for i in range(4)]
        self.prob_f = [sb("probf0", [128, 640])]
        self.h4s = [sb(f"h4s{i}", [128, TC], BF16) for i in range(2)]
        self.h4l = [sb(f"h4l{i}", [128, TC], BF16) for i in range(3)]
        self.xm = [sb(f"xm{i}", [128, TC]) for i in range(3)]
        self.tb = [sb(f"tb{i}", [128, TC]) for i in range(4)]
        self.sml = sb("sml", [128, 64])
        ra = sb("rows_all", [2, 5120])
        slot = lambda col, rows, width=TC: ra[0:rows, col:col + width]
        self.strow = [slot(0, 1), slot(512, 1), slot(1024, 1), slot(1536, 1)]
        self.rows2 = slot(2048, 2)
        self.ones_row = slot(2560, 1)
        self.nrm2 = [slot(3072, 2)]
        self.rn2 = [slot(3584, 2)]
        self.grow = slot(4096, 2, 1024)
        self.nm_ = [sb(f"nm{i}", [128, 1]) for i in range(4)]
        self.acc = [sb(f"acc{i}", [128, 2]) for i in range(2)]
        self.ident = sb("identc", [128, 128])
        self.rmatT = sb("rmatTc", [128, 128])
        self.masks = sb("masksc", [128, 256])
        self.selpair = sb("selpairc", [128, 2])
        self.sel2bc = sb("sel2bcc", [2, 128])
        self.ones_col = sb("onescolc", [128, 1])
        self.epscol = sb("epscolc", [128, 4])
        self.wkc = sb("wkc", [128, 32], BF16)
        self.wke = sb("wke", [32, 64], BF16)
        self.wvc = sb("wvc", [128, 32], BF16)
        self.wve = sb("wve", [32, 64], BF16)
        self.msml = sb("s_msml", [128, 400])
        self.state = sb("s_state", [128, 256])
        self.id32 = sb("s_id32", [32, 32])
        self.pf = [self.stack.enter_context(self.nc.psum_tensor(f"pf{i}", [128, 512], F32))
                   for i in range(8)]
        print(f"SBUF bytes/partition: {self.sb_bytes}")

    def psum(self):
        self.psum_rr = (self.psum_rr + 1) % 6
        return self.pf[self.psum_rr], ("pf", self.psum_rr)

    def rot(self, name, n):
        i = self.sbufs.get(name, -1) + 1
        self.sbufs[name] = i
        return i % n

    def mm(self, out_ap, lhsT_ap, rhs_ap, reads, writes, start=True, stop=True):
        return self.em.op("tensor",
                          lambda e, o=out_ap, l=lhsT_ap, r=rhs_ap, s=start, t=stop:
                          e.matmul(o, l, r, start=s, stop=t), reads, writes)

    def tr(self, out_ap, in_ap, reads, writes):
        return self.em.op("tensor",
                          lambda e, o=out_ap, i=in_ap, d=self.ident:
                          e.transpose(o, i, d[:, :]), reads, writes)

    def act(self, out_ap, in_ap, func, reads, writes, bias=0.0, scale=1.0, accum=None):
        return self.em.op("scalar",
                          lambda e, o=out_ap, i=in_ap, f=func, b=bias, s=scale, a=accum:
                          e.activation(o, i, f, bias=b, scale=s, accum_out=a),
                          reads, writes)

    def v(self, fn, reads, writes):
        return self.em.op("vector", fn, reads, writes)

    def g(self, fn, reads, writes):
        return self.em.op("gpsimd", fn, reads, writes)

    def ld(self, stream, eng, out_ap, in_ap, reads, writes):
        return self.em.dma(stream, eng,
                           lambda e, o=out_ap, i=in_ap: e.dma_start(out=o, in_=i),
                           reads, writes)

    def load_chunk(self, dram, dname, c, lead=True):
        for p in range(NP):
            rd = [(dname, p, c)]
            if lead and c > 0:
                rd.append((dname, p, c - 1))
            xa = self.xa[p]
            if lead and c == 0:
                self.v(lambda e, a=xa: e.memset(a[:, 3:4], 0.0), [], [("xa", p)])
            if lead and c > 0:
                self.ld("xload", "sync", xa[:, 3:4 + TC],
                        dram[p * 128:(p + 1) * 128, c * TC - 1:(c + 1) * TC],
                        rd, [("xa", p)])
            else:
                self.ld("xload", "sync", xa[:, 4:4 + TC],
                        dram[p * 128:(p + 1) * 128, c * TC:(c + 1) * TC],
                        rd, [("xa", p)])

    def ln_stats(self, shifted):
        xa = self.xa
        ps_s, ks = self.psum()
        ps_q, kq = self.psum()
        st = self.strow
        for p in range(NP):
            off = 3 if (shifted and p >= NP // 2) else 4
            self.mm(ps_s[0:1, :], self.ones_col[:, :], xa[p][:, off:off + TC],
                    [("xa", p), ("c", "ones_col")], [ks],
                    start=(p == 0), stop=(p == NP - 1))
            r = self.rot("scr", 3)
            sq = self.scr[r]
            self.act(sq[:, :], xa[p][:, off:off + TC], AF.Square,
                     [("xa", p)], [("scr", r)])
            self.mm(ps_q[0:1, :], self.ones_col[:, :], sq[:, :],
                    [("scr", r), ("c", "ones_col")], [kq],
                    start=(p == 0), stop=(p == NP - 1))
        self.v(lambda e, o=st[0], i=ps_s: e.tensor_scalar_mul(o[:, :], i[0:1, :], 1.0 / DM),
               [ks], [("st", 0)])
        self.v(lambda e, o=st[1], i=ps_q: e.tensor_scalar_mul(o[:, :], i[0:1, :], 1.0 / DM),
               [kq], [("st", 1)])
        self.v(lambda e, o=st[2], a=st[0]: e.tensor_mul(o[:, :], a[:, :], a[:, :]),
               [("st", 0)], [("st", 2)])
        self.v(lambda e, o=st[2], a=st[1], b=st[2]: e.tensor_sub(o[:, :], a[:, :], b[:, :]),
               [("st", 1), ("st", 2)], [("st", 2)])
        self.act(st[3][:, :], st[2][:, :], AF.Sqrt,
                 [("st", 2), ("c", "epscol")], [("st", 3)],
                 bias=self.epscol[0:1, 0:1])
        self.v(lambda e, o=st[2], i=st[3]: e.reciprocal(o[:, :], i[:, :]),
               [("st", 3)], [("st", 2)])
        self.v(lambda e, o=self.rows2, a=st[0], b=st[2]:
               e.scalar_tensor_tensor(o[0:1, :], a[:, :], -1.0, b[:, :], ALU.mult, ALU.mult),
               [("st", 0), ("st", 2)], [("rows2",)])

    def ln_apply(self, shifted, dst, dst_regions, grow_ap, grow_reg):
        for p in range(NP):
            off = 3 if (shifted and p >= NP // 2) else 4
            ps_S, kS = self.psum()
            ps_B, kB = self.psum()
            if grow_ap is None:
                self.mm(ps_S[:, :], self.ones_row[0:1, 0:128], self.strow[2][:, :],
                        [("st", 2), ("onesrow",)], [kS])
                self.mm(ps_B[:, :], self.ones_row[0:1, 0:128], self.rows2[0:1, :],
                        [("rows2",), ("onesrow",)], [kB])
            else:
                self.mm(ps_S[:, :], grow_ap[0:1, 128 * p:128 * (p + 1)],
                        self.strow[2][:, :],
                        [("st", 2), grow_reg], [kS])
                self.mm(ps_B[:, :], grow_ap[0:2, 128 * p:128 * (p + 1)],
                        self.rows2[0:2, :],
                        [("rows2",), grow_reg, ("rows2ones",)], [kB])
            r = self.rot("scr", 3)
            t = self.scr[r]
            self.v(lambda e, o=t, a=self.xa[p], b=ps_S, f=off:
                   e.tensor_mul(o[:, :], a[:, f:f + TC], b[:, :]),
                   [("xa", p), kS], [("scr", r)])
            self.v(lambda e, o=dst[p], a=t, b=ps_B:
                   e.tensor_add(o[:, :], a[:, :], b[:, :]),
                   [("scr", r), kB], dst_regions(p))

    def store_chunk(self, dram, dname, src_ap, p, c, src_reg):
        self.ld("store", "sync", dram[p * 128:(p + 1) * 128, c * TC:(c + 1) * TC],
                src_ap, [src_reg], [(dname, p, c)])

    def emit_outer_ln(self, xsrc, xsrc_name, xi, xi_name):
        for c in range(NCH):
            par = c % 2
            self.load_chunk(xsrc, xsrc_name, c, lead=False)
            self.ln_stats(False)
            for p in range(NP):
                ps_S, kS = self.psum()
                ps_B, kB = self.psum()
                self.mm(ps_S[:, :], self.ones_row[0:1, 0:128], self.strow[2][:, :],
                        [("st", 2), ("onesrow",)], [kS])
                self.mm(ps_B[:, :], self.ones_row[0:1, 0:128], self.rows2[0:1, :],
                        [("rows2",), ("onesrow",)], [kB])
                rs = self.rot("scr", 3)
                t2 = self.scr[rs]
                r = self.rot("xm", 2)
                t = self.xm[r]
                self.v(lambda e, o=t2, a=self.xa[p], b=ps_S:
                       e.tensor_mul(o[:, :], a[:, 4:4 + TC], b[:, :]),
                       [("xa", p), kS], [("scr", rs)])
                self.v(lambda e, o=t, a=t2, b=ps_B:
                       e.tensor_add(o[:, :], a[:, :], b[:, :]),
                       [("scr", rs), kB], [("xm", r)])
                self.store_chunk(xi, xi_name, t[:, :], p, c, ("xm", r))

    def emit_macaron_add2(self, xin, xin_name, xi, xi_name, xout, xout_name, scale):
        for c in range(NCH):
            for p in range(NP):
                r2 = self.rot("xm", 3)
                a2 = self.xm[r2]
                self.ld("xload", "sync", a2[:, :],
                        xi[p * 128:(p + 1) * 128, c * TC:(c + 1) * TC],
                        [(xi_name, p, c)], [("xm", r2)])
                rs = self.rot("scr", 3)
                t = self.scr[rs]
                self.ld("xload", "sync", t[:, :],
                        xin[p * 128:(p + 1) * 128, c * TC:(c + 1) * TC],
                        [(xin_name, p, c)], [("scr", rs)])
                self.v(lambda e, o=t, a=a2, s=scale:
                       e.scalar_tensor_tensor(o[:, :], a[:, :], s, o[:, :],
                                              ALU.mult, ALU.add),
                       [("xm", r2), ("scr", rs)], [("scr", rs)])
                self.store_chunk(xout, xout_name, t[:, :], p, c, ("scr", rs))

    def emit_local_stack(self, cfg, w, xi_a, name_a, xi_b, name_b, tabs, h4d):
        for l in range(cfg.ndepth):
            self.emit_local_layer(cfg, l, w, xi_a, name_a, xi_b, name_b, tabs, h4d)
        return xi_a, name_a

    def emit_local_layer(self, cfg, l, w, cur, curn, nxt, nxtn, tabs, h4d):
        nq, L = cfg.nq, cfg.L
        for p in range(NP):
            self.ld("wload", "gpsimd", self.wk[p][:, 0:3 * cfg.inner],
                    w["wqkv"][l, 128 * p:128 * (p + 1), :], [], [("wk", p)])
        for p in range(nq):
            self.ld("wload", "gpsimd",
                    self.wk[p][:, 3 * cfg.inner:3 * cfg.inner + 1024],
                    w["wo"][l, 128 * p:128 * (p + 1), :], [], [("wk", p)])
        self.ld("wload", "gpsimd", self.sml[:, 0:54], w["smalls"][l, :, :],
                [], [("sml",)])
        self.ld("wload", "gpsimd", self.grow[:, :], w["smallrows"][l, :, :],
                [], [("grow",)])

        for c in range(NCH):
            par = c % 2
            self.load_chunk(cur, curn, c, lead=True)
            for i in range(4):
                self.ld("xload", "sync", self.tb[i][:, :],
                        tabs[i, :, c * TC:(c + 1) * TC], [], [("tb", i)])
            for p in range(NP):
                self.g(lambda e, o=self.xab[p], i=self.xa[p]:
                       e.tensor_copy(o[:, 3:4 + TC], i[:, 3:4 + TC]),
                       [("xa", p)], [("xab", p)])
            for m in range(cfg.nm):
                ps, kps = self.psum()
                for p in range(NP):
                    off = 3 if p >= NP // 2 else 4
                    self.mm(ps[:, :], self.wk[p][:, 128 * m:128 * (m + 1)],
                            self.xab[p][:, off:off + TC],
                            [("wk", p), ("xab", p)], [kps],
                            start=(p == 0), stop=(p == NP - 1))
                if m < 2 * nq:
                    self.qk_tile(cfg, c, m, ps, kps)
                else:
                    self.v_tile(cfg, c, m, ps, kps)
            for pr in range(cfg.npair):
                po, kpo = self.pf[6 + pr % 2], ("pf", 6 + pr % 2)
                for h in (2 * pr, 2 * pr + 1):
                    self.attn_head(cfg, c, h, po, kpo)
                self.act(self.bfB[pr], po[:, :], AF.Copy, [kpo], [("xab", pr)])
            for mo in range(NP):
                ps, kps = self.psum()
                for p in range(nq):
                    self.mm(ps[:, :],
                            self.wk[p][:, 3 * cfg.inner + 128 * mo:
                                       3 * cfg.inner + 128 * (mo + 1)],
                            self.bfB[p],
                            [("wk", p), ("xab", p)], [kps],
                            start=(p == 0), stop=(p == nq - 1))
                r = self.rot("xm", 2)
                t = self.xm[r]
                self.v(lambda e, o=t, a=ps, b=self.xa[mo]:
                       e.tensor_add(o[:, :], a[:, :], b[:, 4:4 + TC]),
                       [kps, ("xa", mo)], [("xm", r)])
                self.store_chunk(nxt, nxtn, t[:, :], mo, c, ("xm", r))

        for p in range(NP):
            self.ld("wload", "gpsimd", self.wk[p][:, 0:2 * HID],
                    w["w1"][l, 128 * p:128 * (p + 1), :], [], [("wk", p)])
        for c in range(NCH):
            par = c % 2
            self.load_chunk(nxt, nxtn, c, lead=True)
            self.ln_stats(True)
            self.ln_apply(True, self.bfA, lambda p: [("bfA", p)],
                          self.grow[:, :], ("grow",))
            for j in range(NJ):
                mj = min(128, HID - 128 * j)
                psg, kpg = self.psum()
                for p in range(NP):
                    self.mm(psg[0:mj, :], self.wk[p][:, HID + 128 * j:HID + 128 * j + mj],
                            self.bfA[p][:, :],
                            [("wk", p), ("bfA", p)], [kpg],
                            start=(p == 0), stop=(p == NP - 1))
                rg = self.rot("scr", 3)
                self.act(self.scr[rg][0:mj, :], psg[0:mj, :], AF.Gelu,
                         [kpg, ("sml",)], [("scr", rg)],
                         bias=self.sml[0:mj, 24 + j:25 + j])
                psa, kpa = self.psum()
                for p in range(NP):
                    self.mm(psa[0:mj, :], self.wk[p][:, 128 * j:128 * j + mj],
                            self.bfA[p][:, :],
                            [("wk", p), ("bfA", p)], [kpa],
                            start=(p == 0), stop=(p == NP - 1))
                rh = self.rot("h4s", 2)
                self.v(lambda e, o=self.h4s[rh], a=psa, b=self.scr[rg],
                       s=self.sml[0:mj, 2 + j:3 + j], mj=mj:
                       e.scalar_tensor_tensor(o[0:mj, :], a[0:mj, :], s, b[0:mj, :],
                                              ALU.add, ALU.mult),
                       [kpa, ("scr", rg), ("sml",)], [("h4s", rh)])
                self.ld("store", "sync",
                        h4d[128 * j:128 * j + mj, c * TC:(c + 1) * TC],
                        self.h4s[rh][0:mj, :], [("h4s", rh)], [("H4", j, c)])

        for k in range(NJ):
            kk = min(128, HID - 128 * k)
            self.ld("wload", "gpsimd",
                    self.wk[k // 3][0:kk, (k % 3) * 1024:(k % 3) * 1024 + 1024],
                    w["w2"][l, 128 * k:128 * k + kk, :], [], [("wk", k // 3)])
        for c in range(NCH):
            banks = self.psum8()
            for k in range(NJ):
                kk = min(128, HID - 128 * k)
                rh = self.rot("h4l", 3)
                self.ld("xload", "sync", self.h4l[rh][0:kk, :],
                        h4d[128 * k:128 * k + kk, c * TC:(c + 1) * TC],
                        [("H4", k, c)], [("h4l", rh)])
                for mo in range(NP):
                    ps, kps = banks[mo]
                    self.mm(ps[:, :],
                            self.wk[k // 3][0:kk, (k % 3) * 1024 + 128 * mo:
                                            (k % 3) * 1024 + 128 * (mo + 1)],
                            self.h4l[rh][0:kk, :],
                            [("wk", k // 3), ("h4l", rh)], [kps],
                            start=(k == 0), stop=(k == NJ - 1))
            for mo in range(NP):
                ps, kps = banks[mo]
                r = self.rot("xm", 3)
                t = self.xm[r]
                self.ld("xload", "sync", t[:, :],
                        nxt[mo * 128:(mo + 1) * 128, c * TC:(c + 1) * TC],
                        [(nxtn, mo, c)], [("xm", r)])
                self.v(lambda e, o=t, a=ps, s=self.sml[:, 46 + mo:47 + mo]:
                       e.scalar_tensor_tensor(o[:, :], a[:, :], s, o[:, :],
                                              ALU.add, ALU.add),
                       [kps, ("xm", r), ("sml",)], [("xm", r)])
                self.store_chunk(cur, curn, t[:, :], mo, c, ("xm", r))

    def psum8(self):
        return [(self.pf[i], ("pf", i)) for i in range(NP)]

    def qk_tile(self, cfg, c, m, ps, kps):
        nq = cfg.nq
        is_q = m < nq
        r = self.rot("qkf", 2)
        qkf = self.qkf[r]
        self.act(qkf[:, :], ps[:, :], AF.Copy, [kps], [("qkf", r)])
        rs = self.rot("scr", 3)
        sq = self.scr[rs]
        self.act(sq[:, :], qkf[:, :], AF.Square, [("qkf", r)], [("scr", rs)])
        ps2, k2 = self.psum()
        self.mm(ps2[0:2, :], self.selpair[:, :], sq[:, :],
                [("scr", rs), ("c", "selpair")], [k2])
        rn = self.rot("nrm", 1)
        self.act(self.nrm2[rn][:, :], ps2[0:2, :], AF.Sqrt, [k2], [("nrm2", rn)])
        self.v(lambda e, o=self.nrm2[rn]: e.tensor_scalar_max(o[:, :], o[:, :], 1e-12),
               [("nrm2", rn)], [("nrm2", rn)])
        self.v(lambda e, o=self.rn2[rn], i=self.nrm2[rn]: e.reciprocal(o[:, :], i[:, :]),
               [("nrm2", rn)], [("rn2", rn)])
        psb, kb = self.psum()
        self.mm(psb[:, :], self.sel2bc[:, :], self.rn2[rn][:, :],
                [("rn2", rn), ("c", "sel2bc")], [kb])
        rh = self.rot("scr", 3)
        qh = self.scr[rh]
        scol = self.sml[:, 0:1] if is_q else self.sml[:, 1:2]
        self.v(lambda e, o=qh, a=qkf, s=scol, b=psb:
               e.scalar_tensor_tensor(o[:, :], a[:, :], s, b[:, :], ALU.mult, ALU.mult),
               [("qkf", r), kb, ("sml",)], [("scr", rh)])
        psr, krr = self.psum()
        self.mm(psr[:, :], self.rmatT[:, :], qh[:, :],
                [("scr", rh), ("c", "rmatT")], [krr])
        ics, iss = (0, 1) if is_q else (2, 3)
        rt1 = self.rot("qkf", 2)
        t1 = self.qkf[rt1]
        self.v(lambda e, o=t1, a=qh, b=self.tb[ics]:
               e.tensor_mul(o[:, :], a[:, :], b[:, :]),
               [("scr", rh), ("tb", ics)], [("qkf", rt1)])
        rt2 = self.rot("scr", 3)
        t2 = self.scr[rt2]
        self.v(lambda e, o=t2, a=psr, b=self.tb[iss]:
               e.tensor_mul(o[:, :], a[:, :], b[:, :]),
               [krr, ("tb", iss)], [("scr", rt2)])
        if is_q:
            dst, dreg = self.bfA[m][:, :], ("bfA", m)
        else:
            p = m - nq
            dst, dreg = self.kr[p][:, (c % 2) * 512:(c % 2) * 512 + 512], ("kr", p, c % 2)
        self.v(lambda e, o=dst, a=t1, b=t2: e.tensor_add(o, a[:, :], b[:, :]),
               [("qkf", rt1), ("scr", rt2)], [dreg])

    def v_tile(self, cfg, c, m, ps, kps):
        nq = cfg.nq
        mv = m - 2 * nq
        rv = self.rot("scr", 3)
        vf = self.scr[rv]
        self.act(vf[:, :], ps[:, :], AF.Copy, [kps], [("scr", rv)])
        for t in range(TC // 128):
            ptb, kpt = self.psum()
            self.tr(ptb[:, 0:128], vf[:, 128 * t:128 * (t + 1)],
                    [("scr", rv), ("c", "ident")], [kpt])
            gidx = c * (TC // 128) + t
            if cfg.inner == 1024:
                s = gidx % 6
                dst = self.vtm[s][:, 128 * mv:128 * (mv + 1)]
                dreg = ("vtm", s)
            else:
                s = gidx % 8
                dst = self.vtm[s // 2][:, (s % 2) * 512 + 128 * mv:
                                      (s % 2) * 512 + 128 * (mv + 1)]
                dreg = ("vtm", s // 2)
            self.act(dst, ptb[:, 0:128], AF.Copy, [kpt], [dreg])

    def attn_head(self, cfg, c, h, po, kpo):
        nq, L = cfg.nq, cfg.L
        kp = h // 2
        base = (h % 2) * 64
        for tt in range(TC // 128):
            gidx = c * (TC // 128) + tt
            kts = list(range(max(0, gidx - L), gidx + 1))
            nk = len(kts)
            pieces = [kts[i:i + 4] for i in range(0, nk, 4)]
            psc = []
            q_ap = self.bfA[kp][base:base + 64, 128 * tt:128 * (tt + 1)]
            qreg = ("bfA", kp)
            for pc in pieces:
                pbk, kbk = self.psum()
                psc.append((pbk, kbk, pc))
                run = [pc[0]]
                for kt in pc[1:] + [None]:
                    if kt is not None and kt % 8 != 0 and kt == run[-1] + 1:
                        run.append(kt)
                        continue
                    a = run[0] % 8
                    col0 = 128 * pc.index(run[0])
                    self.mm(pbk[:, col0:col0 + 128 * len(run)], q_ap,
                            self.kr[kp][base:base + 64, 128 * a:128 * (a + len(run))],
                            [qreg, ("kr", kp, 0), ("kr", kp, 1)], [kbk])
                    if kt is None:
                        break
                    run = [kt]
                for i, kt in enumerate(pc):
                    if kt == gidx:
                        self.v(lambda e, p_=pbk, i_=i, mk=self.masks:
                               e.tensor_add(p_[:, 128 * i_:128 * (i_ + 1)],
                                            p_[:, 128 * i_:128 * (i_ + 1)],
                                            mk[:, 0:128]),
                               [kbk, ("c", "masks")], [kbk])
                    elif kt == gidx - L:
                        self.v(lambda e, p_=pbk, i_=i, mk=self.masks:
                               e.tensor_add(p_[:, 128 * i_:128 * (i_ + 1)],
                                            p_[:, 128 * i_:128 * (i_ + 1)],
                                            mk[:, 128:256]),
                               [kbk, ("c", "masks")], [kbk])
            dbg = getattr(self, "dbg", None)
            dbg = dbg if (dbg is not None and dbg.get("chtt") == (c, h, tt)) else None
            if dbg is not None and "sc" in dbg:
                for ip_, (pbk_, kbk_, pc_) in enumerate(psc):
                    rs_ = self.rot("scr", 3)
                    self.act(self.scr[rs_][:, 0:128 * len(pc_)], pbk_[:, 0:128 * len(pc_)],
                             AF.Copy, [kbk_], [("scr", rs_)])
                    self.ld("store", "sync", dbg["sc"][:, 512 * ip_:512 * ip_ + 128 * len(pc_)],
                            self.scr[rs_][:, 0:128 * len(pc_)], [("scr", rs_)], [("dbgsc", ip_)])
            rn = self.rot("nmr", 4)
            nm = self.nm_[rn]
            if len(psc) == 1:
                pbk, kbk, pc = psc[0]
                self.v(lambda e, o=nm, i=pbk, w=128 * len(pc):
                       e.reduce_max(o[:, :], i[:, 0:w], axis=AX.X),
                       [kbk], [("nm", rn)])
            else:
                rn2 = self.rot("nmr", 4)
                nm2 = self.nm_[rn2]
                (p0, k0, c0), (p1, k1, c1) = psc
                self.v(lambda e, o=nm, i=p0, w=128 * len(c0):
                       e.reduce_max(o[:, :], i[:, 0:w], axis=AX.X),
                       [k0], [("nm", rn)])
                self.v(lambda e, o=nm2, i=p1, w=128 * len(c1):
                       e.reduce_max(o[:, :], i[:, 0:w], axis=AX.X),
                       [k1], [("nm", rn2)])
                self.v(lambda e, o=nm, a=nm2: e.tensor_max(o[:, :], o[:, :], a[:, :]),
                       [("nm", rn), ("nm", rn2)], [("nm", rn)])
            self.v(lambda e, o=nm: e.tensor_scalar_mul(o[:, :], o[:, :], -8.0),
                   [("nm", rn)], [("nm", rn)])
            rp = self.rot("prob", 1)
            pf = self.prob_f[rp]
            ra = self.rot("accr", 2)
            acc = self.acc[ra]
            col = 0
            for ip, (pbk, kbk, pc) in enumerate(psc):
                wd = 128 * len(pc)
                self.act(pf[:, col:col + wd], pbk[:, 0:wd], AF.Exp,
                         [kbk, ("nm", rn)], [("pf", rp), ("acc", ra, ip)],
                         bias=nm[:, :], scale=8.0, accum=acc[:, ip:ip + 1])
                col += wd
            if len(psc) == 2:
                self.v(lambda e, o=acc: e.tensor_add(o[:, 0:1], o[:, 0:1], o[:, 1:2]),
                       [("acc", ra, 0), ("acc", ra, 1)], [("acc", ra, 0)])
            self.v(lambda e, o=acc: e.reciprocal(o[:, 0:1], o[:, 0:1]),
                   [("acc", ra, 0)], [("acc", ra, 0)])
            self.v(lambda e, o=pf, s=acc, w=128 * nk:
                   e.tensor_scalar_mul(o[:, 0:w], o[:, 0:w], s[:, 0:1]),
                   [("pf", rp), ("acc", ra, 0)], [("pf", rp)])
            if dbg is not None:
                self.ld("store", "sync", dbg["pf"][:, 0:128 * nk], pf[:, 0:128 * nk],
                        [("pf", rp)], [("dbgpf",)])
                if "nm" in dbg:
                    self.ld("store", "sync", dbg["nm"][:, :], nm[:, :],
                            [("nm", rn)], [("dbgnm",)])
                    self.ld("store", "sync", dbg["ac"][:, :], acc[:, 0:2],
                            [("acc", ra, 0)], [("dbgac",)])
            for i, kt in enumerate(kts):
                ptb, kpt = self.psum()
                self.tr(ptb[:, 0:128], pf[:, 128 * i:128 * (i + 1)],
                        [("pf", rp), ("c", "ident")], [kpt])
                rT = self.rot("pT", 4)
                self.act(self.pT[rT][:, :], ptb[:, 0:128],
                         AF.Copy, [kpt], [("pT", rT)])
                if cfg.inner == 1024:
                    s = kt % 6
                    v_ap = self.vtm[s][:, 64 * h:64 * (h + 1)]
                    vreg = ("vtm", s)
                else:
                    s = kt % 8
                    v_ap = self.vtm[s // 2][:, (s % 2) * 512 + 64 * h:
                                           (s % 2) * 512 + 64 * (h + 1)]
                    vreg = ("vtm", s // 2)
                self.mm(po[base:base + 64, 128 * tt:128 * (tt + 1)],
                        v_ap, self.pT[rT][:, :],
                        [vreg, ("pT", rT)], [kpo],
                        start=(i == 0), stop=(i == nk - 1))


    def emit_l0a(self, w, xin, xin_name, xi, xi_name, xout, xout_name,
                 kexp_d, vtm_d, qd_d):
        NQH = 16
        for p in range(NP):
            self.ld("wload", "gpsimd", self.wk[p][:, 0:3072],
                    w["cw"][128 * p:128 * (p + 1), :], [], [("wk", p)])
            self.ld("wload", "gpsimd", self.wk[p][:, 3072:4096],
                    w["cp"][128 * p:128 * (p + 1), :], [], [("wk", p)])
        self.ld("wload", "gpsimd", self.sml[:, 0:32], w["smalls"][:, :], [], [("sml",)])
        self.ld("wload", "gpsimd", self.wkc[:, :], w["wk_c"][:, :], [], [("wkc",)])
        self.ld("wload", "gpsimd", self.wke[:, :], w["wk_e"][:, :], [], [("wke",)])
        self.ld("wload", "gpsimd", self.wvc[:, :], w["wv_c"][:, :], [], [("wvc",)])
        self.ld("wload", "gpsimd", self.wve[:, :], w["wv_e"][:, :], [], [("wve",)])
        for c in range(NCH):
            self.load_chunk(xi, xi_name, c, lead=False)
            for p in range(NP):
                self.g(lambda e, o=self.xab[p], i=self.xa[p]:
                       e.tensor_copy(o[:, 4:4 + TC], i[:, 4:4 + TC]),
                       [("xa", p)], [("xab", p)])
            for m in range(24):
                ps, kps = self.psum()
                for p in range(NP):
                    self.mm(ps[:, :], self.wk[p][:, 128 * m:128 * (m + 1)],
                            self.xab[p][:, 4:4 + TC],
                            [("wk", p), ("xab", p)], [kps],
                            start=(p == 0), stop=(p == NP - 1))
                bias = self.sml[:, m:m + 1]
                if m < 8:
                    self.v(lambda e, o=self.bfA[m], i=ps, b=bias:
                           e.tensor_scalar_add(o[:, :], i[:, :], b),
                           [kps, ("sml",)], [("bfA", m)])
                elif m < 16:
                    rk = self.rot("h4s", 2)
                    kf = self.h4s[rk]
                    self.v(lambda e, o=kf, i=ps, b=bias:
                           e.tensor_scalar_add(o[:, :], i[:, :], b),
                           [kps, ("sml",)], [("h4s", rk)])
                    for hh in range(2):
                        h = 2 * (m - 8) + hh
                        psl, kl = self.psum()
                        self.mm(psl[0:32, :], self.wkc[64 * hh:64 * hh + 64, :],
                                kf[64 * hh:64 * hh + 64, :],
                                [("h4s", rk), ("wkc",)], [kl])
                        rl = self.rot("h4l", 3)
                        self.act(self.h4l[rl][0:32, :], psl[0:32, :], AF.Copy,
                                 [kl], [("h4l", rl)])
                        pse, ke = self.psum()
                        self.mm(pse[0:64, :], self.wke[:, :], self.h4l[rl][0:32, :],
                                [("h4l", rl), ("wke",)], [ke])
                        self.act(self.kr[h // 2][64 * (h % 2):64 * (h % 2) + 64, 0:TC],
                                 pse[0:64, :], AF.Copy, [ke], [("kr", h // 2, 0)])
                else:
                    rk = self.rot("h4s", 2)
                    vf_ = self.h4s[rk]
                    self.v(lambda e, o=vf_, i=ps, b=bias:
                           e.tensor_scalar_add(o[:, :], i[:, :], b),
                           [kps, ("sml",)], [("h4s", rk)])
                    for hh in range(2):
                        h = 2 * (m - 16) + hh
                        psl, kl = self.psum()
                        self.mm(psl[0:32, :], self.wvc[64 * hh:64 * hh + 64, :],
                                vf_[64 * hh:64 * hh + 64, :],
                                [("h4s", rk), ("wvc",)], [kl])
                        rl = self.rot("h4l", 3)
                        self.act(self.h4l[rl][0:32, :], psl[0:32, :], AF.Copy,
                                 [kl], [("h4l", rl)])
                        pse, ke = self.psum()
                        self.mm(pse[0:64, :], self.wve[:, :], self.h4l[rl][0:32, :],
                                [("h4l", rl), ("wve",)], [ke])
                        rs = self.rot("scr", 3)
                        self.act(self.scr[rs][0:64, :], pse[0:64, :], AF.Copy,
                                 [ke], [("scr", rs)])
                        for t in range(4):
                            ptb, kpt = self.psum()
                            self.em.op("tensor",
                                       lambda e, o=ptb[:, 0:64],
                                       i=self.scr[rs][0:64, 128 * t:128 * (t + 1)],
                                       d=self.ident[0:64, 0:64]:
                                       e.transpose(o, i, d),
                                       [("scr", rs), ("c", "ident")], [kpt])
                            self.act(self.vtm[t][:, 64 * h:64 * (h + 1)], ptb[:, 0:64],
                                     AF.Copy, [kpt], [("vtm", t)])
            for p in range(NP):
                self.ld("store", "sync", qd_d[128 * p:128 * (p + 1), c * TC:(c + 1) * TC],
                        self.bfA[p][:, :], [("bfA", p)], [("QD", p, c)])
                self.ld("store", "sync", kexp_d[128 * p:128 * (p + 1), c * TC:(c + 1) * TC],
                        self.kr[p][:, 0:TC], [("kr", p, 0)], [("KE", p, c)])
            for t in range(4):
                self.ld("store", "sync",
                        vtm_d[c * TC + 128 * t:c * TC + 128 * (t + 1), :],
                        self.vtm[t][:, :], [("vtm", t)], [("VT", c, t)])
        m_all, l_all = self.qkf[0], self.qkf[1]
        for g in range(32):
            cdiag = g // 4
            for p in range(NP):
                self.ld("xload", "sync", self.bfA[p][:, 0:128],
                        qd_d[128 * p:128 * (p + 1), 128 * g:128 * (g + 1)],
                        [("QD", p, g // 4)], [("bfA", p)])
            for pr in range(NP):
                for pas in (1, 2):
                    for kc in range(cdiag + 1):
                        wd = TC if kc < cdiag else (g % 4 + 1) * 128
                        par = kc % 2
                        self.ld("xload", "sync", self.kr[pr][:, par * TC:par * TC + wd],
                                kexp_d[128 * pr:128 * (pr + 1), kc * TC:kc * TC + wd],
                                [("KE", pr, kc)], [("kr", pr, par)])
                        for hh in range(2):
                            h = 2 * pr + hh
                            col = 16 * g + h
                            pbk, kbk = self.psum()
                            self.mm(pbk[:, 0:wd], self.bfA[pr][64 * hh:64 * hh + 64, 0:128],
                                    self.kr[pr][64 * hh:64 * hh + 64, par * TC:par * TC + wd],
                                    [("bfA", pr), ("kr", pr, par)], [kbk])
                            if kc == cdiag:
                                self.v(lambda e, p_=pbk, w_=wd, mk=self.masks:
                                       e.tensor_add(p_[:, w_ - 128:w_], p_[:, w_ - 128:w_],
                                                    mk[:, 0:128]),
                                       [kbk, ("c", "masks")], [kbk])
                            if pas == 1:
                                rn = self.rot("nmr", 4)
                                nm = self.nm_[rn]
                                self.v(lambda e, o=nm, i=pbk, w_=wd:
                                       e.reduce_max(o[:, :], i[:, 0:w_], axis=AX.X),
                                       [kbk], [("nm", rn)])
                                if kc == 0:
                                    self.v(lambda e, o=m_all, i=nm, cl=col:
                                           e.tensor_copy(o[:, cl:cl + 1], i[:, :]),
                                           [("nm", rn)], [("mall", col)])
                                else:
                                    self.v(lambda e, o=m_all, i=nm, cl=col:
                                           e.tensor_max(o[:, cl:cl + 1], o[:, cl:cl + 1],
                                                        i[:, :]),
                                           [("nm", rn), ("mall", col)], [("mall", col)])
                            else:
                                rn = self.rot("nmr", 4)
                                nm = self.nm_[rn]
                                ra = self.rot("accr", 2)
                                acc = self.acc[ra]
                                self.v(lambda e, o=nm, i=m_all, cl=col:
                                       e.tensor_scalar_mul(o[:, :], i[:, cl:cl + 1], -0.125),
                                       [("mall", col)], [("nm", rn)])
                                rp = self.rot("prob", 1)
                                pf = self.prob_f[rp]
                                self.act(pf[:, 0:wd], pbk[:, 0:wd], AF.Exp,
                                         [kbk, ("nm", rn)], [("pf", rp), ("acc", ra, 0)],
                                         bias=nm[:, :], scale=0.125, accum=acc[:, 0:1])
                                if kc == 0:
                                    self.v(lambda e, o=l_all, i=acc, cl=col:
                                           e.tensor_copy(o[:, cl:cl + 1], i[:, 0:1]),
                                           [("acc", ra, 0)], [("lall", col)])
                                else:
                                    self.v(lambda e, o=l_all, i=acc, cl=col:
                                           e.tensor_add(o[:, cl:cl + 1], o[:, cl:cl + 1],
                                                        i[:, 0:1]),
                                           [("acc", ra, 0), ("lall", col)], [("lall", col)])
            self.v(lambda e, o=l_all, cl=16 * g:
                   e.reciprocal(o[:, cl:cl + 16], o[:, cl:cl + 16]),
                   [("lall", 16 * g + h_) for h_ in range(16)],
                   [("lall", 16 * g + h_) for h_ in range(16)])
            for pr in range(NP):
                pos = [(self.pf[6], ("pf", 6)), (self.pf[7], ("pf", 7))]
                for kc in range(cdiag + 1):
                    wd = TC if kc < cdiag else (g % 4 + 1) * 128
                    par = kc % 2
                    self.ld("xload", "sync", self.kr[pr][:, par * TC:par * TC + wd],
                            kexp_d[128 * pr:128 * (pr + 1), kc * TC:kc * TC + wd],
                            [("KE", pr, kc)], [("kr", pr, par)])
                    nvt = (wd + 127) // 128
                    for t in range(nvt):
                        self.ld("xload", "sync",
                                self.vtm[t][:, 128 * par:128 * par + 128],
                                vtm_d[kc * TC + 128 * t:kc * TC + 128 * (t + 1),
                                      128 * pr:128 * (pr + 1)],
                                [("VT", kc, t)], [("vtm", t, par), ("vtm", t)])
                    for hh in range(2):
                        h = 2 * pr + hh
                        col = 16 * g + h
                        pbk, kbk = self.psum()
                        self.mm(pbk[:, 0:wd], self.bfA[pr][64 * hh:64 * hh + 64, 0:128],
                                self.kr[pr][64 * hh:64 * hh + 64, par * TC:par * TC + wd],
                                [("bfA", pr), ("kr", pr, par)], [kbk])
                        if kc == cdiag:
                            self.v(lambda e, p_=pbk, w_=wd, mk=self.masks:
                                   e.tensor_add(p_[:, w_ - 128:w_], p_[:, w_ - 128:w_],
                                                mk[:, 0:128]),
                                   [kbk, ("c", "masks")], [kbk])
                        rn = self.rot("nmr", 4)
                        nm = self.nm_[rn]
                        self.v(lambda e, o=nm, i=m_all, cl=col:
                               e.tensor_scalar_mul(o[:, :], i[:, cl:cl + 1], -0.125),
                               [("mall", col)], [("nm", rn)])
                        rp = self.rot("prob", 1)
                        pf = self.prob_f[rp]
                        self.act(pf[:, 0:wd], pbk[:, 0:wd], AF.Exp,
                                 [kbk, ("nm", rn)], [("pf", rp)],
                                 bias=nm[:, :], scale=0.125)
                        self.v(lambda e, o=pf, s=l_all, cl=col, w_=wd:
                               e.tensor_scalar_mul(o[:, 0:w_], o[:, 0:w_], s[:, cl:cl + 1]),
                               [("pf", rp), ("lall", col)], [("pf", rp)])
                        for t in range(nvt):
                            wt = 128
                            ptb, kpt = self.psum()
                            self.tr(ptb[:, 0:128], pf[:, 128 * t:128 * t + wt],
                                    [("pf", rp), ("c", "ident")], [kpt])
                            rT = self.rot("pT", 4)
                            self.act(self.pT[rT][0:wt, :], ptb[0:wt, 0:128],
                                     AF.Copy, [kpt], [("pT", rT)])
                            po, kpo = pos[hh]
                            self.mm(po[0:64, 0:128],
                                    self.vtm[t][0:wt, 128 * par + 64 * hh:
                                                128 * par + 64 * hh + 64],
                                    self.pT[rT][0:wt, :],
                                    [("vtm", t, par), ("pT", rT)], [kpo],
                                    start=(kc == 0 and t == 0),
                                    stop=(kc == cdiag and t == nvt - 1))
                for hh in range(2):
                    po, kpo = pos[hh]
                    self.act(self.bfB[pr][64 * hh:64 * hh + 64,
                                          128 * (g % 4):128 * (g % 4 + 1)],
                             po[0:64, 0:128], AF.Copy, [kpo], [("xab", pr)])
            if g % 4 == 3:
                c = g // 4
                for mo in range(NP):
                    ps, kps = self.psum()
                    for p in range(NP):
                        self.mm(ps[:, :], self.wk[p][:, 3072 + 128 * mo:
                                                     3072 + 128 * (mo + 1)],
                                self.bfB[p],
                                [("wk", p), ("xab", p)], [kps],
                                start=(p == 0), stop=(p == NP - 1))
                    r = self.rot("xm", 2)
                    t = self.xm[r]
                    self.ld("xload", "sync", t[:, :],
                            xin[mo * 128:(mo + 1) * 128, c * TC:(c + 1) * TC],
                            [(xin_name, mo, c)], [("xm", r)])
                    self.v(lambda e, o=t, a=ps, s=self.sml[:, 24 + mo:25 + mo]:
                           e.scalar_tensor_tensor(o[:, :], a[:, :], s, o[:, :],
                                                  ALU.add, ALU.add),
                           [kps, ("xm", r), ("sml",)], [("xm", r)])
                    self.store_chunk(xout, xout_name, t[:, :], mo, c, ("xm", r))


    def emit_mamba(self, w, xin, xin_name, xi, xi_name, xout, xout_name,
                   xin_d, z_d, xc_d, dt_d, y_d, phases=(1, 2, 3, 4)):
        NDT = 16
        for p in range(NP):
            self.ld("wload", "gpsimd", self.wk[p][:, 0:4096],
                    w["winT"][128 * p:128 * (p + 1), :], [], [("wk", p)])
        for k in range(16):
            if k < 8:
                dst, reg = self.wk[k][:, 4096:5120], ("wk", k)
            elif k < 14:
                dst, reg = self.vtm[k - 8][:, :], ("vtm", k - 8)
            else:
                dst, reg = self.kr[k - 8][:, :], ("kr", k - 8, 0)
            self.ld("wload", "gpsimd", dst, w["woutT"][128 * k:128 * (k + 1), :],
                    [], [reg])
        for p in range(16):
            self.ld("wload", "gpsimd", self.bfA[p // 5][:, 96 * (p % 5):96 * (p % 5) + 96],
                    w["wxT"][128 * p:128 * (p + 1), :], [], [("bfA", p // 5)])
        for p in range(16):
            self.ld("wload", "gpsimd",
                    self.wk[p // 2][0:64, 5120 + 128 * (p % 2):5120 + 128 * (p % 2) + 128],
                    w["wdtT"][:, 128 * p:128 * (p + 1)], [], [("wk", p // 2)])
        self.ld("wload", "gpsimd", self.msml[:, 256:400], w["msml"][:, :],
                [], [("msml", "b")])
        self.ld("xload", "sync", self.scr[0][:, 0:256], w["alog"][:, :],
                [], [("scr", 0)])
        self.act(self.msml[:, 0:256], self.scr[0][:, 0:256], AF.Exp,
                 [("scr", 0)], [("msml", "A")])
        self.v(lambda e, o=self.msml: e.tensor_scalar_mul(o[:, 0:256], o[:, 0:256], -1.0),
               [("msml", "A")], [("msml", "A")])
        self.v(lambda e, o=self.state: e.memset(o[:, :], 0.0), [],
               [("state", d_, n_) for d_ in range(16) for n_ in range(16)])
        da4 = [self.kr[i][:, :].bitcast(F32) for i in range(4)]
        db4 = [self.kr[4][:, :].bitcast(F32), self.kr[5][:, :].bitcast(F32),
               self.xa[6][:, 0:TC], self.xa[7][:, 0:TC]]
        hbuf = [self.xm[1][:, :], self.xm[2][:, :]]
        daR = [("kr", i, 0) for i in range(4)]
        dbR = [("kr", 4, 0), ("kr", 5, 0), ("xa", 6), ("xa", 7)]
        hR = [("xm", 1), ("xm", 2)]
        Bb4, Cb4 = self.tb, [self.qkf[0], self.qkf[1], self.scr[0], self.scr[1]]
        BbR = [("tb", i) for i in range(4)]
        CbR = [("qkf", 0), ("qkf", 1), ("scr", 0), ("scr", 1)]
        for c in range(NCH):
            self.load_chunk(xi, xi_name, c, lead=False)
            for p in range(NP):
                self.g(lambda e, o=self.xab[p], i=self.xa[p]:
                       e.tensor_copy(o[:, 4:4 + TC], i[:, 4:4 + TC]),
                       [("xa", p)], [("xab", p)])
            for m in range(32):
                ps, kps = self.psum()
                for p in range(NP):
                    self.mm(ps[:, :], self.wk[p][:, 128 * m:128 * (m + 1)],
                            self.xab[p][:, 4:4 + TC],
                            [("wk", p), ("xab", p)], [kps],
                            start=(p == 0), stop=(p == NP - 1))
                r = self.rot("xm", 3)
                t = self.xm[r]
                self.act(t[:, :], ps[:, :], AF.Copy, [kps], [("xm", r)])
                dstd = xin_d if m < 16 else z_d
                row = 128 * (m if m < 16 else m - 16)
                self.ld("store", "sync", dstd[row:row + 128, c * TC:(c + 1) * TC],
                        t[:, :], [("xm", r)], [("XZ", m, c)])
            if 2 not in phases:
                continue
            pdbl, kdbl = self.psum()
            for dt_ in range(NDT):
                rx = self.rot("xa6", 6)
                xt = self.xa[rx]
                if c == 0:
                    self.v(lambda e, a=xt: e.memset(a[:, 0:3], 0.0), [], [("xa", rx)])
                    self.ld("xload", "sync", xt[:, 3:3 + TC],
                            xin_d[128 * dt_:128 * (dt_ + 1), c * TC:(c + 1) * TC],
                            [("XZ", dt_, c)], [("xa", rx)])
                else:
                    self.ld("xload", "sync", xt[:, 0:3 + TC],
                            xin_d[128 * dt_:128 * (dt_ + 1), c * TC - 3:(c + 1) * TC],
                            [("XZ", dt_, c), ("XZ", dt_, c - 1)], [("xa", rx)])
                r1 = self.rot("xm", 3)
                acc = self.xm[r1]
                self.v(lambda e, o=acc, i=xt, w0=self.msml[:, 272 + 4 * dt_:273 + 4 * dt_],
                       bc=self.msml[:, 256 + dt_:257 + dt_]:
                       e.tensor_scalar(o[:, :], i[:, 0:TC], w0, bc, ALU.mult, ALU.add),
                       [("xa", rx), ("msml", "b")], [("xm", r1)])
                for kk in range(1, 4):
                    self.v(lambda e, o=acc, i=xt, a=acc,
                           wc=self.msml[:, 272 + 4 * dt_ + kk:273 + 4 * dt_ + kk], k_=kk:
                           e.scalar_tensor_tensor(o[:, :], i[:, k_:k_ + TC], wc, a[:, :],
                                                  ALU.mult, ALU.add),
                           [("xa", rx), ("msml", "b"), ("xm", r1)], [("xm", r1)])
                rc = self.rot("xa6", 6)
                xct = self.xa[rc]
                self.act(xct[:, 0:TC], acc[:, :], AF.Silu, [("xm", r1)], [("xa", rc)])
                self.ld("store", "sync", xc_d[128 * dt_:128 * (dt_ + 1), c * TC:(c + 1) * TC],
                        xct[:, 0:TC], [("xa", rc)], [("XC", dt_, c)])
                rb = self.rot("h4s", 2)
                xcb = self.h4s[rb]
                self.g(lambda e, o=xcb, i=xct: e.tensor_copy(o[:, :], i[:, 0:TC]),
                       [("xa", rc)], [("h4s", rb)])
                self.mm(pdbl[0:96, :],
                        self.bfA[dt_ // 5][:, 96 * (dt_ % 5):96 * (dt_ % 5) + 96],
                        xcb[:, :], [("bfA", dt_ // 5), ("h4s", rb)], [kdbl],
                        start=(dt_ == 0), stop=(dt_ == NDT - 1))
            rb = self.rot("h4s", 2)
            dtin = self.h4s[rb]
            bcr = self.prob_f[0][0:32, 0:TC]
            self.act(dtin[0:64, :], pdbl[0:64, :], AF.Copy, [kdbl], [("h4s", rb)])
            self.act(bcr, pdbl[64:96, :], AF.Copy, [kdbl], [("pf", 0)])
            for dt_ in range(NDT):
                psd, kd = self.psum()
                self.mm(psd[:, :],
                        self.wk[dt_ // 2][0:64, 5120 + 128 * (dt_ % 2):
                                          5120 + 128 * (dt_ % 2) + 128],
                        dtin[0:64, :], [("wk", dt_ // 2), ("h4s", rb)], [kd])
                bdt = self.msml[:, 336 + dt_:337 + dt_]
                r1 = self.rot("xm", 3)
                rl_ = self.xm[r1]
                self.act(rl_[:, :], psd[:, :], AF.Relu, [kd, ("msml", "b")],
                         [("xm", r1)], bias=bdt)
                rx = self.rot("xa6", 6)
                ab = self.xa[rx]
                self.act(ab[:, 0:TC], psd[:, :], AF.Abs, [kd, ("msml", "b")],
                         [("xa", rx)], bias=bdt)
                self.act(ab[:, 0:TC], ab[:, 0:TC], AF.Exp, [("xa", rx)], [("xa", rx)],
                         scale=-1.0)
                self.act(ab[:, 0:TC], ab[:, 0:TC], AF.Ln, [("xa", rx)], [("xa", rx)],
                         bias=1.0)
                self.v(lambda e, o=ab, a=rl_: e.scalar_tensor_tensor(
                       o[:, 0:TC], o[:, 0:TC], 1e-4, a[:, :], ALU.add, ALU.add),
                       [("xa", rx), ("xm", r1)], [("xa", rx)])
                self.ld("store", "sync", dt_d[128 * dt_:128 * (dt_ + 1), c * TC:(c + 1) * TC],
                        ab[:, 0:TC], [("xa", rx)], [("DT", dt_, c)])
            if 3 not in phases:
                continue
            for ng in range(4):
                for i in range(4):
                    n = 4 * ng + i
                    bcr = self.prob_f[0][0:32, 0:TC]
                    psb, kb = self.psum()
                    self.mm(psb[:, :],
                            self.id32[:, n:n + 1].broadcast_to((32, 128)),
                            bcr, [("pf", 0), ("c", "id32")], [kb])
                    self.act(Bb4[i][:, :], psb[:, :], AF.Copy, [kb], [BbR[i]])
                    psc2, kc2 = self.psum()
                    self.mm(psc2[:, :],
                            self.id32[:, 16 + n:17 + n].broadcast_to((32, 128)),
                            bcr, [("pf", 0), ("c", "id32")], [kc2])
                    self.act(Cb4[i][:, :], psc2[:, :], AF.Copy, [kc2], [CbR[i]])
                for dt_ in range(NDT):
                    rx = self.rot("xa6", 6)
                    dtt = self.xa[rx]
                    self.ld("xload", "sync", dtt[:, 0:TC],
                            dt_d[128 * dt_:128 * (dt_ + 1), c * TC:(c + 1) * TC],
                            [("DT", dt_, c)], [("xa", rx)])
                    rc = self.rot("xa6", 6)
                    xct = self.xa[rc]
                    self.ld("xload", "sync", xct[:, 0:TC],
                            xc_d[128 * dt_:128 * (dt_ + 1), c * TC:(c + 1) * TC],
                            [("XC", dt_, c)], [("xa", rc)])
                    ru = self.rot("xa6", 6)
                    ut = self.xa[ru]
                    self.g(lambda e, o=ut, a=dtt, b=xct:
                           e.tensor_mul(o[:, 0:TC], a[:, 0:TC], b[:, 0:TC]),
                           [("xa", rx), ("xa", rc)], [("xa", ru)])
                    ry = 0
                    yp = self.xm[0]
                    for i in range(4):
                        n = 4 * ng + i
                        acol = self.msml[:, 16 * dt_ + n:16 * dt_ + n + 1]
                        self.v(lambda e, o=db4[i], a=dtt, s=acol:
                               e.tensor_scalar(o, a[:, 0:TC], s, -10.0,
                                               ALU.mult, ALU.max),
                               [("xa", rx), ("msml", "A")], [dbR[i]])
                        self.act(da4[i], db4[i], AF.Exp, [dbR[i]], [daR[i]])
                        self.g(lambda e, o=db4[i], a=ut, b=Bb4[i]:
                               e.tensor_mul(o, a[:, 0:TC], b[:, :]),
                               [("xa", ru), BbR[i]], [dbR[i]])
                        rh = self.rot("hb", 2)
                        scol = self.state[:, 16 * dt_ + n:16 * dt_ + n + 1]
                        self.v(lambda e, o=hbuf[rh], a=da4[i], b=db4[i], s=scol:
                               e.tensor_tensor_scan(o, a, b, s, ALU.mult, ALU.add),
                               [daR[i], dbR[i], ("state", dt_, n)], [hR[rh]])
                        self.g(lambda e, o=self.state, i=hbuf[rh], cl=16 * dt_ + n:
                               e.tensor_copy(o[:, cl:cl + 1], i[:, TC - 1:TC]),
                               [hR[rh]], [("state", dt_, n)])
                        self.v(lambda e, o=da4[i], a=hbuf[rh], b=Cb4[i]:
                               e.tensor_mul(o, a, b[:, :]),
                               [hR[rh], CbR[i]], [daR[i]])
                        if i == 0:
                            self.v(lambda e, o=yp, i_=da4[0]: e.tensor_copy(o[:, :], i_),
                                   [daR[0]], [("xm", ry)])
                        else:
                            self.v(lambda e, o=yp, i_=da4[i]:
                                   e.tensor_add(o[:, :], o[:, :], i_),
                                   [daR[i], ("xm", ry)], [("xm", ry)])
                    if ng == 0:
                        self.ld("store", "sync",
                                y_d[128 * dt_:128 * (dt_ + 1), c * TC:(c + 1) * TC],
                                yp[:, :], [("xm", ry)], [("YD", dt_, c)])
                    else:
                        self.em.dma("ysum", "gpsimd",
                                    lambda e, o=y_d[128 * dt_:128 * (dt_ + 1),
                                                    c * TC:(c + 1) * TC], i=yp[:, :]:
                                    e.dma_start(out=o, in_=i, accum_op=ALU.add),
                                    [("xm", ry), ("YD", dt_, c)], [("YD", dt_, c)])
            if 4 not in phases:
                continue
            ps_q, kq = self.psum()
            for dt_ in range(NDT):
                rx = self.rot("xa6", 6)
                yt = self.xa[rx]
                self.ld("xload", "sync", yt[:, 0:TC],
                        y_d[128 * dt_:128 * (dt_ + 1), c * TC:(c + 1) * TC],
                        [("YD", dt_, c)], [("xa", rx)])
                rc = self.rot("xa6", 6)
                xct = self.xa[rc]
                self.ld("xload", "sync", xct[:, 0:TC],
                        xc_d[128 * dt_:128 * (dt_ + 1), c * TC:(c + 1) * TC],
                        [("XC", dt_, c)], [("xa", rc)])
                rz = self.rot("xa6", 6)
                zt = self.xa[rz]
                self.ld("xload", "sync", zt[:, 0:TC],
                        z_d[128 * dt_:128 * (dt_ + 1), c * TC:(c + 1) * TC],
                        [("XZ", 16 + dt_, c)], [("xa", rz)])
                self.act(zt[:, 0:TC], zt[:, 0:TC], AF.Silu, [("xa", rz)], [("xa", rz)])
                dcol = self.msml[:, 352 + dt_:353 + dt_]
                self.v(lambda e, o=yt, a=xct, s=dcol:
                       e.scalar_tensor_tensor(o[:, 0:TC], a[:, 0:TC], s, o[:, 0:TC],
                                              ALU.mult, ALU.add),
                       [("xa", rc), ("msml", "b"), ("xa", rx)], [("xa", rx)])
                self.v(lambda e, o=yt, b=zt:
                       e.tensor_mul(o[:, 0:TC], o[:, 0:TC], b[:, 0:TC]),
                       [("xa", rx), ("xa", rz)], [("xa", rx)])
                self.ld("store", "sync",
                        y_d[128 * dt_:128 * (dt_ + 1), c * TC:(c + 1) * TC],
                        yt[:, 0:TC], [("xa", rx)], [("YD", dt_, c)])
                rs = self.rot("h4s", 2)
                sq = self.h4s[rs].bitcast(F32)[:, 0:256]
                r2 = self.rot("xm", 3)
                sqt = self.xm[r2]
                self.act(sqt[:, :], yt[:, 0:TC], AF.Square, [("xa", rx)], [("xm", r2)])
                self.mm(ps_q[0:1, :], self.ones_col[:, :], sqt[:, :],
                        [("xm", r2), ("c", "ones_col")], [kq],
                        start=(dt_ == 0), stop=(dt_ == NDT - 1))
            st = self.strow
            self.v(lambda e, o=st[0], i=ps_q:
                   e.tensor_scalar_mul(o[:, :], i[0:1, :], 1.0 / 2048.0),
                   [kq], [("st", 0)])
            self.act(st[1][:, :], st[0][:, :], AF.Sqrt,
                     [("st", 0), ("c", "epscol")], [("st", 1)],
                     bias=self.epscol[0:1, 1:2])
            self.v(lambda e, o=st[2], i=st[1]: e.reciprocal(o[:, :], i[:, :]),
                   [("st", 1)], [("st", 2)])
            psr, krr = self.psum()
            self.mm(psr[:, :], self.ones_row[0:1, 0:128], st[2][:, :],
                    [("st", 2), ("onesrow",)], [krr])
            rbc = self.rot("tb4", 4)
            bcst = self.tb[rbc]
            self.act(bcst[:, :], psr[:, :], AF.Copy, [krr], [("tb", rbc)])
            banks = self.psum8()
            for k in range(16):
                rx = self.rot("xa6", 6)
                yt = self.xa[rx]
                self.ld("xload", "sync", yt[:, 0:TC],
                        y_d[128 * k:128 * (k + 1), c * TC:(c + 1) * TC],
                        [("YD", k, c)], [("xa", rx)])
                rb3 = self.rot("h4l", 3)
                y3 = self.h4l[rb3]
                rmw = self.msml[:, 368 + k:369 + k]
                self.v(lambda e, o=y3, a=yt, s=rmw, b=bcst:
                       e.scalar_tensor_tensor(o[:, :], a[:, 0:TC], s, b[:, :],
                                              ALU.mult, ALU.mult),
                       [("xa", rx), ("msml", "b"), ("tb", rbc)], [("h4l", rb3)])
                if k < 8:
                    wap, wreg = self.wk[k][:, 4096:5120], ("wk", k)
                elif k < 14:
                    wap, wreg = self.vtm[k - 8][:, :], ("vtm", k - 8)
                else:
                    wap, wreg = self.kr[k - 8][:, :], ("kr", k - 8, 0)
                for mo in range(NP):
                    ps, kps = banks[mo]
                    self.mm(ps[:, :], wap[:, 128 * mo:128 * (mo + 1)], y3[:, :],
                            [wreg, ("h4l", rb3)], [kps],
                            start=(k == 0), stop=(k == 15))
            for mo in range(NP):
                ps, kps = banks[mo]
                r = self.rot("xm", 3)
                t = self.xm[r]
                self.ld("xload", "sync", t[:, :],
                        xin[mo * 128:(mo + 1) * 128, c * TC:(c + 1) * TC],
                        [(xin_name, mo, c)], [("xm", r)])
                self.v(lambda e, o=t, a=ps:
                       e.tensor_add(o[:, :], a[:, :], o[:, :]),
                       [kps, ("xm", r)], [("xm", r)])
                self.store_chunk(xout, xout_name, t[:, :], mo, c, ("xm", r))



def build_local_program(cfg, fused_scale=0.5, depth=None, debug=False):
    import copy
    if depth is not None:
        cfg = copy.copy(cfg)
        cfg.depth = max(depth, 1)
        cfg.ndepth = depth
    else:
        cfg = copy.copy(cfg)
        cfg.ndepth = cfg.depth
    nc = bass.Bass("TRN2", target_bir_lowering=False)
    xin = nc.dram_tensor("xin", [DM, T], F32, kind="ExternalInput").ap()
    xout = nc.dram_tensor("xout", [DM, T], F32, kind="ExternalOutput").ap()
    w = {}
    w["wqkv"] = nc.dram_tensor("wqkv", [cfg.depth, DM, 3 * cfg.inner], F32,
                               kind="ExternalInput").ap()
    w["wo"] = nc.dram_tensor("wo", [cfg.depth, cfg.inner, DM], F32,
                             kind="ExternalInput").ap()
    w["w1"] = nc.dram_tensor("w1", [cfg.depth, DM, 2 * HID], F32,
                             kind="ExternalInput").ap()
    w["w2"] = nc.dram_tensor("w2", [cfg.depth, HID, DM], F32,
                             kind="ExternalInput").ap()
    w["smalls"] = nc.dram_tensor("smalls", [cfg.depth, 128, 54], F32,
                                 kind="ExternalInput").ap()
    w["smallrows"] = nc.dram_tensor("smallrows", [cfg.depth, 2, DM], F32,
                                    kind="ExternalInput").ap()
    tabs = nc.dram_tensor("tabs", [4, 128, T], F32, kind="ExternalInput").ap()
    cp = const_pack()
    consts = {k: nc.dram_tensor("c_" + k, list(v.shape), F32,
                                kind="ExternalInput").ap() for k, v in cp.items()}
    kind_i = "ExternalOutput" if debug else "Internal"
    xi_a = nc.dram_tensor("xi_a", [DM, T], F32, kind=kind_i).ap()
    xi_b = nc.dram_tensor("xi_b", [DM, T], F32, kind=kind_i).ap()
    h4d = nc.dram_tensor("h4d", [NJ * 128, T], BF16, kind="Internal").ap()

    with ExitStack() as stack:
        mk = MK(nc, stack)
        mk.alloc_common()
        block = stack.enter_context(nc.Block())
        dstmap = {"ident": mk.ident, "rmatT": mk.rmatT, "masks": mk.masks,
                  "selpair": mk.selpair, "sel2bc": mk.sel2bc,
                  "ones_col": mk.ones_col, "epscol": mk.epscol, "id32": mk.id32}
        for k in cp:
            mk.ld("xload", "sync", dstmap[k][:, :], consts[k][:, :], [], [("c", k)])
        mk.em.op("vector", lambda e, o=mk.rows2: e.memset(o[0:2, :], 1.0),
                 [], [("rows2ones",)])
        mk.em.op("vector", lambda e, o=mk.ones_row: e.memset(o[:, :], 1.0),
                 [], [("onesrow",)])
        mk.emit_outer_ln(xin, "XIN", xi_a, "XA")
        fin, finn = mk.emit_local_stack(cfg, w, xi_a, "XA", xi_b, "XB", tabs, h4d)
        mk.emit_macaron_add2(xin, "XIN", fin, finn, xout, "XOUT", fused_scale)
        mk.em.finalize_and_emit(nc, stack, block)
    return nc


def local_in_map(cfg, xin_fm, wqkv, wo, w1, w2, qs, ks, lng, lnb, b1, b2):
    m = {
        "xin": np.ascontiguousarray(xin_fm),
        "wqkv": np.ascontiguousarray(wqkv),
        "wo": np.ascontiguousarray(wo),
        "w1": np.ascontiguousarray(w1),
        "w2": np.ascontiguousarray(w2),
        "smalls": pack_local_smalls(qs, ks, b1, b2),
        "smallrows": np.ascontiguousarray(np.stack([lng, lnb], axis=1)),
        "tabs": rotary_tables(cfg.window),
    }
    for k, v in const_pack().items():
        m["c_" + k] = v
    return m




def build_l0a_program():
    nc = bass.Bass("TRN2", target_bir_lowering=False)
    xin = nc.dram_tensor("xin", [DM, T], F32, kind="ExternalInput").ap()
    xout = nc.dram_tensor("xout", [DM, T], F32, kind="ExternalOutput").ap()
    w = {}
    w["cw"] = nc.dram_tensor("cw", [DM, 3072], F32, kind="ExternalInput").ap()
    w["cp"] = nc.dram_tensor("cp", [DM, 1024], F32, kind="ExternalInput").ap()
    w["smalls"] = nc.dram_tensor("smalls", [128, 32], F32, kind="ExternalInput").ap()
    w["wk_c"] = nc.dram_tensor("wk_c", [128, 32], F32, kind="ExternalInput").ap()
    w["wk_e"] = nc.dram_tensor("wk_e", [32, 64], F32, kind="ExternalInput").ap()
    w["wv_c"] = nc.dram_tensor("wv_c", [128, 32], F32, kind="ExternalInput").ap()
    w["wv_e"] = nc.dram_tensor("wv_e", [32, 64], F32, kind="ExternalInput").ap()
    cp_ = const_pack()
    consts = {k: nc.dram_tensor("c_" + k, list(v.shape), F32,
                                kind="ExternalInput").ap() for k, v in cp_.items()}
    xi_a = nc.dram_tensor("xi_a", [DM, T], F32, kind="Internal").ap()
    kexp_d = nc.dram_tensor("kexp_d", [DM, T], BF16, kind="Internal").ap()
    vtm_d = nc.dram_tensor("vtm_d", [T, DM], BF16, kind="Internal").ap()
    qd_d = nc.dram_tensor("qd_d", [DM, T], BF16, kind="Internal").ap()
    with ExitStack() as stack:
        mk = MK(nc, stack)
        mk.alloc_common()
        block = stack.enter_context(nc.Block())
        dstmap = {"ident": mk.ident, "rmatT": mk.rmatT, "masks": mk.masks,
                  "selpair": mk.selpair, "sel2bc": mk.sel2bc,
                  "ones_col": mk.ones_col, "epscol": mk.epscol, "id32": mk.id32}
        for k in cp_:
            mk.ld("xload", "sync", dstmap[k][:, :], consts[k][:, :], [], [("c", k)])
        mk.em.op("vector", lambda e, o=mk.rows2: e.memset(o[0:2, :], 1.0),
                 [], [("rows2ones",)])
        mk.em.op("vector", lambda e, o=mk.ones_row: e.memset(o[:, :], 1.0),
                 [], [("onesrow",)])
        mk.emit_outer_ln(xin, "XIN", xi_a, "XA")
        mk.emit_l0a(w, xin, "XIN", xi_a, "XA", xout, "XOUT", kexp_d, vtm_d, qd_d)
        mk.em.finalize_and_emit(nc, stack, block)
    return nc


def l0a_in_map(xin_fm, cw, cb, cpw, cpb, wkc, wke, wvc, wve):
    smalls = np.zeros((128, 32), np.float32)
    for mm_ in range(24):
        smalls[:, mm_] = cb[128 * mm_:128 * (mm_ + 1)]
    for p in range(NP):
        smalls[:, 24 + p] = cpb[128 * p:128 * (p + 1)]
    m = {"xin": np.ascontiguousarray(xin_fm),
         "cw": np.ascontiguousarray(cw), "cp": np.ascontiguousarray(cpw),
         "smalls": smalls,
         "wk_c": np.ascontiguousarray(np.concatenate([wkc, wkc], 0)),
         "wk_e": np.ascontiguousarray(wke),
         "wv_c": np.ascontiguousarray(np.concatenate([wvc, wvc], 0)),
         "wv_e": np.ascontiguousarray(wve)}
    for k, v in const_pack().items():
        m["c_" + k] = v
    return m




def build_mamba_program():
    nc = bass.Bass("TRN2", target_bir_lowering=False)
    xin = nc.dram_tensor("xin", [DM, T], F32, kind="ExternalInput").ap()
    xout = nc.dram_tensor("xout", [DM, T], F32, kind="ExternalOutput").ap()
    w = {}
    for nm, shp in [("winT", [DM, 4096]), ("woutT", [2048, 1024]),
                    ("wxT", [2048, 96]), ("wdtT", [64, 2048]),
                    ("alog", [128, 256]), ("msml", [128, 144])]:
        w[nm] = nc.dram_tensor(nm, shp, F32, kind="ExternalInput").ap()
    cp_ = const_pack()
    consts = {k: nc.dram_tensor("c_" + k, list(v.shape), F32,
                                kind="ExternalInput").ap() for k, v in cp_.items()}
    xi_a = nc.dram_tensor("xi_a", [DM, T], F32, kind="Internal").ap()
    import os as _os
    _k = "ExternalOutput" if _os.environ.get("MAMBA_DBG") else "Internal"
    scr_d = {nm: nc.dram_tensor(nm, [2048, T], F32, kind=_k).ap()
             for nm in ["xin_d", "z_d", "xc_d", "dt_d", "y_d"]}
    with ExitStack() as stack:
        mk = MK(nc, stack)
        mk.alloc_common()
        block = stack.enter_context(nc.Block())
        dstmap = {"ident": mk.ident, "rmatT": mk.rmatT, "masks": mk.masks,
                  "selpair": mk.selpair, "sel2bc": mk.sel2bc,
                  "ones_col": mk.ones_col, "epscol": mk.epscol, "id32": mk.id32}
        for k in cp_:
            mk.ld("xload", "sync", dstmap[k][:, :], consts[k][:, :], [], [("c", k)])
        mk.em.op("vector", lambda e, o=mk.rows2: e.memset(o[0:2, :], 1.0),
                 [], [("rows2ones",)])
        mk.em.op("vector", lambda e, o=mk.ones_row: e.memset(o[:, :], 1.0),
                 [], [("onesrow",)])
        mk.emit_outer_ln(xin, "XIN", xi_a, "XA")
        import os as _os2
        _ph = tuple(int(x) for x in _os2.environ.get("MAMBA_PHASES", "1234"))
        mk.emit_mamba(w, xin, "XIN", xi_a, "XA", xout, "XOUT",
                      scr_d["xin_d"], scr_d["z_d"], scr_d["xc_d"],
                      scr_d["dt_d"], scr_d["y_d"], phases=_ph)
        mk.em.finalize_and_emit(nc, stack, block)
    return nc


def mamba_in_map(xin_fm, W_in, W_conv, b_conv, W_x, W_dt, b_dt, A_log, D, rms_w, W_out):
    m = pack_mamba(W_in, W_conv, b_conv, W_x, W_dt, b_dt, A_log, D, rms_w, W_out)
    m["xin"] = np.ascontiguousarray(xin_fm)
    for k, v in const_pack().items():
        m["c_" + k] = v
    return m


_PROGS = {}


def _prog(name):
    if name not in _PROGS:
        import time
        t0 = time.time()
        if name == "l0f":
            _PROGS[name] = build_local_program(L0F, 0.5)
        elif name == "l1f":
            _PROGS[name] = build_local_program(L1F, 0.5)
        elif name == "l0a":
            _PROGS[name] = build_l0a_program()
        elif name == "mamba":
            _PROGS[name] = build_mamba_program()
        print(f"[kernel] built {name} in {time.time() - t0:.1f}s", flush=True)
    return _PROGS[name]


def _launch(name, base_map, xs):
    nc = _prog(name)
    in_maps = []
    for b in range(8):
        m = dict(base_map)
        m["xin"] = xs[b]
        in_maps.append(m)
    res = run_bass_kernel_spmd(nc, in_maps, core_ids=list(range(8)))
    return [np.asarray(r["xout"]) for r in res.results]


def kernel(**inputs):
    g = inputs
    x = np.asarray(g["L0_attn_hidden_states"], np.float32)
    xs = [np.ascontiguousarray(x[b].T) for b in range(8)]

    m_l0f = local_in_map(L0F, xs[0], g["L0_ffn_wqkv"], g["L0_ffn_wo"],
                         g["L0_ffn_w1"], g["L0_ffn_w2"], g["L0_ffn_q_scale"],
                         g["L0_ffn_k_scale"], g["L0_ffn_ln_g"], g["L0_ffn_ln_b"],
                         g["L0_ffn_b1"], g["L0_ffn_b2"])
    del m_l0f["xin"]
    m_l0a = l0a_in_map(xs[0], g["L0_attn_c_attn_w"], g["L0_attn_c_attn_b"],
                       g["L0_attn_c_proj_w"], g["L0_attn_c_proj_b"],
                       g["L0_attn_wk_c"], g["L0_attn_wk_e"],
                       g["L0_attn_wv_c"], g["L0_attn_wv_e"])
    del m_l0a["xin"]
    m_l1f = local_in_map(L1F, xs[0], g["L1_ffn_Wqkv"], g["L1_ffn_Wo"],
                         g["L1_ffn_W1"], g["L1_ffn_W2"], g["L1_ffn_q_scale"],
                         g["L1_ffn_k_scale"], g["L1_ffn_ln_g"], g["L1_ffn_ln_b"],
                         g["L1_ffn_b1"], g["L1_ffn_b2"])
    del m_l1f["xin"]
    m_mb = mamba_in_map(xs[0], g["L1_attn_W_in"], g["L1_attn_W_conv"],
                        g["L1_attn_b_conv"], g["L1_attn_W_x"], g["L1_attn_W_dt"],
                        g["L1_attn_b_dt"], g["L1_attn_A_log"], g["L1_attn_D"],
                        g["L1_attn_rms_w"], g["L1_attn_W_out"])
    del m_mb["xin"]

    xs = _launch("l0f", m_l0f, xs)
    xs = _launch("l0a", m_l0a, xs)
    xs = _launch("l0f", m_l0f, xs)
    xs = _launch("l1f", m_l1f, xs)
    xs = _launch("mamba", m_mb, xs)
    xs = _launch("l1f", m_l1f, xs)
    return np.stack([xs[b].T for b in range(8)], axis=0)
